# version 01
# baseline (speedup 1.0000x reference)
import numpy as np
from contextlib import ExitStack

import concourse.bass as bass
import concourse.mybir as mybir
from concourse.bass_utils import run_bass_kernel_spmd

F32 = mybir.dt.float32
BF16 = mybir.dt.bfloat16
I32 = mybir.dt.int32
AF = mybir.ActivationFunctionType
ALU = mybir.AluOpType

P = 128
G = 6
G1W = 4
G2W = 3
LR = 16
NAD = 3
EPS = 1e-5
LNBIAS = 1e-38
THD2 = 6.25
NPOOL = 5
_DETECT_RACES = True


def full_dims():
    return dict(B=2, T=4096, H=4096, S=1024, F=9984, R=512, NPASS=2,
                FBC=13, NCORES=8)


def derived(d):
    d = dict(d)
    d["KH"] = d["H"] // P
    d["SB"] = d["S"] // P
    d["FBT"] = d["F"] // P
    d["NFC"] = d["FBT"] // d["FBC"]
    d["ROWS_CORE"] = d["B"] * d["T"] // d["NCORES"]
    assert d["ROWS_CORE"] == d["R"] * d["NPASS"]
    return d


class Ev:

    def __init__(self, known=None):
        self.cnt = {}
        self.ev = {}
        self.known = known
        self.semname = {}
        self.sems_by_name = {}
        self.missing = set()

    def reg(self, sem, name):
        self.semname[id(sem)] = name
        self.sems_by_name[name] = sem
        self.cnt[id(sem)] = 0
        return sem

    def inc(self, instr, sem, n, label=None):
        k = id(sem)
        self.cnt[k] += n
        if label is not None:
            assert label not in self.ev, f"dup label {label}"
            self.ev[label] = (self.semname[k], self.cnt[k])
        return instr.then_inc(sem, n)

    def alias(self, new, old):
        self.ev[new] = self.ev[old]

    def wait(self, eng, label):
        if label is None:
            return
        rec = self.ev.get(label)
        if rec is None and self.known is not None:
            rec = self.known.get(label)
        if rec is None:
            if self.known is None:
                self.missing.add(label)
                return
            raise KeyError(f"unknown event label {label}")
        semname, v = rec
        eng.wait_ge(self.sems_by_name[semname], v)


def build_program(d, known_ev=None):
    d = derived(d)
    KH, SB, FBT, FBC, NFC = d["KH"], d["SB"], d["FBT"], d["FBC"], d["NFC"]
    R, NPASS, H = d["R"], d["NPASS"], d["H"]
    CB = KH
    POOL_E = max(KH * P, G1W * SB * P, FBC * P * 2, H)

    nc = bass.Bass("TRN2", target_bir_lowering=False, detect_race_conditions=_DETECT_RACES)

    xt = nc.dram_tensor("xt", [NPASS, P, KH * R], F32, kind="ExternalInput").ap()
    upw = nc.dram_tensor("upw", [SB, P, KH * P], F32, kind="ExternalInput").ap()
    gatew = nc.dram_tensor("gatew", [SB, P, KH * P], F32, kind="ExternalInput").ap()
    apg1 = nc.dram_tensor("apg1", [CB, P, G1W * SB * P], F32, kind="ExternalInput").ap()
    apg2 = nc.dram_tensor("apg2", [CB, P, G2W * SB * P], F32, kind="ExternalInput").ap()
    gwp = nc.dram_tensor("gwp", [FBT, P, KH * P], F32, kind="ExternalInput").ap()
    uwp = nc.dram_tensor("uwp", [FBT, P, KH * P], F32, kind="ExternalInput").ap()
    dwp = nc.dram_tensor("dwp", [NFC, CB, P, FBC * P], F32, kind="ExternalInput").ap()
    wtp = nc.dram_tensor("wtp", [CB, P, KH * P], F32, kind="ExternalInput").ap()
    atp = nc.dram_tensor("atp", [NAD, P, KH * LR], F32, kind="ExternalInput").ap()
    btp = nc.dram_tensor("btp", [NAD, LR, H], F32, kind="ExternalInput").ap()
    biasp = nc.dram_tensor("biasp", [P, CB], F32, kind="ExternalInput").ap()
    lpp = nc.dram_tensor("lpp", [P, SB], F32, kind="ExternalInput").ap()
    dfp = nc.dram_tensor("dfp", [P, G], F32, kind="ExternalInput").ap()
    lsip = nc.dram_tensor("lsip", [1, 4], I32, kind="ExternalInput").ap()
    bidp = nc.dram_tensor("bidp", [1, 1], F32, kind="ExternalInput").ap()
    yout = nc.dram_tensor("y", [NPASS, P, CB * R], F32, kind="ExternalOutput").ap()

    ev = Ev(known=known_ev)

    with ExitStack() as st:
        off = [20480]

        boff, bend = {}, {}

        def alloc(name, shape, dt, at=None):
            nbytes = int(np.prod(shape[1:])) * mybir.dt.size(dt)
            a = off[0] if at is None else at
            if at is None:
                off[0] = (a + nbytes + 31) // 32 * 32
            t = nc.alloc_sbuf_tensor_at(name, shape, dt, offset=a)
            boff[name] = a
            bend[name] = a + nbytes
            return t

        acc = alloc("acc", [P, CB * R], F32)
        xb = alloc("xb", [P, KH * R], BF16)
        pool = [alloc(f"wp{i}", [P, POOL_E], BF16) for i in range(NPOOL)]
        hb0 = alloc("hb0", [P, FBC * R], BF16)
        hb1 = alloc("hb1", [P, FBC * R], BF16)
        assert 3 * SB * R * 2 <= (bend["hb1"] - boff["hb0"])
        tmp = [alloc(f"tmp{i}", [P, SB * R], BF16, at=boff["hb0"] + i * SB * R * 2)
               for i in range(3)]
        ab2 = alloc("ab2", [P, KH * LR], BF16, at=boff["hb0"])
        bb2 = alloc("bb2", [LR, H], BF16,
                    at=boff["hb0"] + ((KH * LR * 2 + 31) // 32 * 32))
        aub = alloc("aub", [P, SB * R], BF16)
        u0 = off[0]
        usz = [0]

        def ualloc(name, shape, dt, uoff):
            nbytes = int(np.prod(shape[1:])) * mybir.dt.size(dt)
            usz[0] = max(usz[0], uoff + nbytes)
            return alloc(name, shape, dt, at=u0 + uoff)

        rb = R * 4
        d32 = [ualloc(f"d32_{i}", [P, R], F32, i * rb) for i in range(2)]
        s32 = [ualloc(f"s32_{i}", [P, R], F32, 2 * rb + i * rb) for i in range(2)]
        lgsb = [ualloc(f"lg_{i}", [P, R], F32, 4 * rb + i * rb) for i in range(2)]
        ebs = [ualloc(f"eb_{i}", [P, R], BF16, 6 * rb + i * rb // 2) for i in range(2)]
        kps = ualloc("kps", [P, R], BF16, 7 * rb)
        negsb = [ualloc(f"ng_{i}", [P, R], BF16, 7 * rb + rb // 2 + i * rb // 2)
                 for i in range(2)]
        sg32 = ualloc("sg32", [P, R], F32, 0)
        sqb = [ualloc(f"sqb{i}", [P, R], BF16, rb + i * rb // 2) for i in range(2)]
        mrep = ualloc("mrep", [P, R], F32, 2 * rb)
        rrep = ualloc("rrep", [P, R], F32, 3 * rb)
        mean32 = ualloc("mean32", [1, R], F32, 4 * rb)
        aux1 = ualloc("aux1", [1, R], F32, 5 * rb)
        aux2 = ualloc("aux2", [1, R], F32, 5 * rb + R * 4 if False else 5 * rb + 2048)
        scrF = ualloc("scrF", [P, R], F32, 7 * rb)
        off[0] = (u0 + usz[0] + 31) // 32 * 32

        apb = alloc("apb", [P, KH * LR], BF16)
        bpb = alloc("bpb", [LR, H], BF16)
        pb = alloc("pb", [LR, R], BF16)
        onesb = alloc("onesb", [P, 1], BF16)
        onesf = alloc("onesf", [1, P], F32)
        cln = alloc("cln", [P, 1], F32)
        ceps = alloc("ceps", [P, 1], F32)
        cthr = alloc("cthr", [P, 1], F32)
        negdf = alloc("negdf", [P, G], F32)
        df_sb = alloc("df_sb", [P, G], F32)
        lp_sb = alloc("lp_sb", [P, SB], F32)
        bias_sb = alloc("bias_sb", [P, CB], F32)
        lsi_sb = alloc("lsi_sb", [1, 4], I32)
        lsf_sb = alloc("lsf_sb", [1, 4], F32)
        bid_sb = alloc("bid_sb", [1, 1], F32)
        cvec = alloc("cvec", [1, 4], F32)
        svec = alloc("svec", [1, 8], F32)
        mvec = alloc("mvec", [1, 4], F32)
        mrep4 = alloc("mrep4", [P, 4], F32)
        assert off[0] <= 229376, f"SBUF overflow: {off[0]}"

        pA = [st.enter_context(nc.psum_tensor(f"pA{i}", [P, R], F32)) for i in range(2)]
        pG = [st.enter_context(nc.psum_tensor(f"pG{i}", [P, R], F32)) for i in range(2)]
        pU = [st.enter_context(nc.psum_tensor(f"pU{i}", [P, R], F32)) for i in range(2)]
        pS = [st.enter_context(nc.psum_tensor(f"pS{i}", [P, R], F32)) for i in range(2)]

        swq = ev.reg(st.enter_context(nc.semaphore("swq")), "swq")
        sxq = ev.reg(st.enter_context(nc.semaphore("sxq")), "sxq")
        sout = ev.reg(st.enter_context(nc.semaphore("sout")), "sout")
        smm = ev.reg(st.enter_context(nc.semaphore("smm")), "smm")
        sev = ev.reg(st.enter_context(nc.semaphore("sev")), "sev")
        sea = ev.reg(st.enter_context(nc.semaphore("sea")), "sea")

        block = st.enter_context(nc.Block())

        fill_seq = []
        for k in range(NAD):
            fill_seq.append((f"at{k}", atp[k]))
        for k in range(NAD):
            fill_seq.append((f"bt{k}", btp[k]))
        for p_ in range(NPASS):
            for s_ in range(SB):
                fill_seq.append((f"up_p{p_}_{s_}", upw[s_]))
            for s_ in range(SB):
                fill_seq.append((f"g1w_p{p_}_{s_}", gatew[s_]))
            for c in range(CB):
                fill_seq.append((f"a1_p{p_}_{c}", apg1[c]))
            for s_ in range(SB):
                fill_seq.append((f"g2w_p{p_}_{s_}", gatew[s_]))
            for c in range(CB):
                fill_seq.append((f"a2_p{p_}_{c}", apg2[c]))
            for fc in range(NFC):
                for j in range(FBC):
                    fill_seq.append((f"gw_p{p_}_{fc}_{j}", gwp[fc * FBC + j]))
                    fill_seq.append((f"uw_p{p_}_{fc}_{j}", uwp[fc * FBC + j]))
                for cp in range(CB // 2):
                    fill_seq.append((f"dw_p{p_}_{fc}_{cp}",
                                     dwp[fc, 2 * cp:2 * cp + 2]
                                     .rearrange("c p f -> p c f")))
            for c in range(CB):
                fill_seq.append((f"wt_p{p_}_{c}", wtp[c]))
        fill_slot = {lab: i % NPOOL for i, (lab, _) in enumerate(fill_seq)}

        def wslot(lab):
            return pool[fill_slot[lab]]

        seqA = []
        for p_ in range(NPASS):
            seqA += [f"au_p{p_}_{s_}" for s_ in range(SB)]
            seqA += [f"gp1_p{p_}_{s_}" for s_ in range(SB)]
            seqA += [f"G1_p{p_}_{c}" for c in range(CB)]
            seqA += [f"gp2_p{p_}_{s_}" for s_ in range(SB)]
            seqA += [f"G2_p{p_}_{c}" for c in range(CB)]
            for fc in range(NFC):
                seqA += [f"D_p{p_}_{fc}_{c}" for c in range(CB)]
            seqA += [f"F_p{p_}_{c}" for c in range(CB)]
        idxA = {lab: i for i, lab in enumerate(seqA)}

        seqG, seqU, seqS0, seqS1 = ["mrep4"], [], [], []
        for p_ in range(NPASS):
            for fc in range(NFC):
                seqG += [f"g_p{p_}_{fc}_{j}" for j in range(FBC)]
                seqU += [f"u_p{p_}_{fc}_{j}" for j in range(FBC)]
            seqS0 += [f"sum_p{p_}_0", f"mrep_p{p_}_0", f"pmm_p{p_}",
                      f"sum_p{p_}_1", f"mrep_p{p_}_1"]
            seqS1 += [f"sq_p{p_}_0", f"rrep_p{p_}_0", f"sq_p{p_}_1",
                      f"rrep_p{p_}_1"]
        seqG_all = ["mrep4"] + seqG[1:] if False else seqG
        idxG = {lab: i for i, lab in enumerate(seqG)}
        idxU = {lab: i for i, lab in enumerate(seqU)}
        idxS0 = {lab: i for i, lab in enumerate(seqS0)}
        idxS1 = {lab: i for i, lab in enumerate(seqS1)}

        def bankA(lab):
            return pA[idxA[lab] % 2]

        def bank_wait(eng, lab, seq, idx, depth=2):
            i = idx[lab]
            if i >= depth:
                ev.wait(eng, f"cons_{seq[i - depth]}")

        gate_chunks = []
        for p_ in range(NPASS):
            for gp in (1, 2):
                for s_ in range(SB):
                    gate_chunks.append((p_, gp, s_))
        gate_q = {t: i for i, t in enumerate(gate_chunks)}

        def ln_mm(t, p_, which):
            ev.wait(t, f"lncast_p{p_}_{which}")
            bank_wait(t, f"sum_p{p_}_{which}", seqS0, idxS0, depth=1)
            bank_wait(t, f"sq_p{p_}_{which}", seqS1, idxS1, depth=1)
            for c in range(CB):
                ins = t.matmul(pS[0][0:1, :R], onesb[:, :],
                               xb[:, c * R:(c + 1) * R],
                               start=(c == 0), stop=(c == CB - 1))
                if c == CB - 1:
                    ev.inc(ins, smm, 1, f"sum_p{p_}_{which}")
                ev.wait(t, f"sqa_p{p_}_{which}_{c}")
                ins = t.matmul(pS[1][0:1, :R], onesb[:, :], sqb[c % 2][:, :R],
                               start=(c == 0), stop=(c == CB - 1))
                ev.inc(ins, smm, 1, f"sqmm_p{p_}_{which}_{c}")
            ev.alias(f"sq_p{p_}_{which}", f"sqmm_p{p_}_{which}_{CB-1}")
            ev.wait(t, f"lnsmall_p{p_}_{which}")
            bank_wait(t, f"mrep_p{p_}_{which}", seqS0, idxS0, depth=1)
            ins = t.matmul(pS[0][:, :R], onesf[0:1, :], mean32[0:1, :R],
                           start=True, stop=True)
            ev.inc(ins, smm, 1, f"mrep_p{p_}_{which}")
            bank_wait(t, f"rrep_p{p_}_{which}", seqS1, idxS1, depth=1)
            ins = t.matmul(pS[1][:, :R], onesf[0:1, :], aux2[0:1, :R],
                           start=True, stop=True)
            ev.inc(ins, smm, 1, f"rrep_p{p_}_{which}")

        def ln_vec(v, p_, which):
            ev.wait(v, f"sum_p{p_}_{which}")
            ins = v.tensor_scalar(mean32[0:1, :R], pS[0][0:1, :R], 1.0 / H, None,
                                  op0=ALU.mult)
            ev.inc(ins, sev, 1, f"cons_sum_p{p_}_{which}")
            ev.wait(v, f"sq_p{p_}_{which}")
            ins = v.tensor_scalar(aux1[0:1, :R], pS[1][0:1, :R], 1.0 / H, None,
                                  op0=ALU.mult)
            ev.inc(ins, sev, 1, f"cons_sq_p{p_}_{which}")
            v.wait_ge(sev, ev.ev[f"cons_sq_p{p_}_{which}"][1])
            ins = v.tensor_tensor(aux2[0:1, :R], mean32[0:1, :R], mean32[0:1, :R],
                                  op=ALU.mult)
            ev.inc(ins, sev, 1, f"m2_p{p_}_{which}")
            v.wait_ge(sev, ev.ev[f"m2_p{p_}_{which}"][1])
            ins = v.tensor_tensor(aux1[0:1, :R], aux1[0:1, :R], aux2[0:1, :R],
                                  op=ALU.subtract)
            ev.inc(ins, sev, 1, f"var_p{p_}_{which}")
            ev.wait(v, f"mrep_p{p_}_{which}")
            ins = v.tensor_copy(mrep[:, :R], pS[0][:, :R])
            ev.inc(ins, sev, 1, f"cons_mrep_p{p_}_{which}")
            ev.wait(v, f"rrep_p{p_}_{which}")
            ins = v.tensor_copy(rrep[:, :R], pS[1][:, :R])
            ev.inc(ins, sev, 1, f"cons_rrep_p{p_}_{which}")
            acc3 = acc[:, :].rearrange("p (c r) -> p c r", c=CB)
            mr3 = mrep[:, :R].unsqueeze(1).broadcast_to([P, CB, R])
            rr3 = rrep[:, :R].unsqueeze(1).broadcast_to([P, CB, R])
            v.tensor_tensor(acc3, acc3, mr3, op=ALU.subtract)
            ins = v.tensor_tensor(acc3, acc3, rr3, op=ALU.mult)
            ev.inc(ins, sev, 1,
                   f"ln1_norm_p{p_}" if which == 0 else f"ln2_done_p{p_}")

        def ln_scalar(sc, p_, which):
            if which == 0:
                ev.wait(sc, f"cons_D_p{p_}_{NFC-1}_{CB-1}")
                ev.wait(sc, f"u_p{p_}_{NFC-1}_{FBC-1}")
            else:
                ev.wait(sc, f"evf_p{p_}_{CB-1}")
                ev.wait(sc, f"pmm_p{p_}")
                ev.wait(sc, f"F_p{p_}_{CB-1}")
            ins = sc.copy(xb[:, :CB * R], acc[:, :])
            ev.inc(ins, sea, 1, f"lncast_p{p_}_{which}")
            for c in range(CB):
                if c >= 2:
                    ev.wait(sc, f"sqmm_p{p_}_{which}_{c-2}")
                ins = sc.activation(sqb[c % 2][:, :R], acc[:, c * R:(c + 1) * R],
                                    AF.Square)
                ev.inc(ins, sea, 1, f"sqa_p{p_}_{which}_{c}")
            ev.wait(sc, f"var_p{p_}_{which}")
            ins = sc.activation(aux2[0:1, :R], aux1[0:1, :R], AF.Ln,
                                bias=ceps[0:1, :])
            ev.inc(ins, sea, 1, f"lnln_p{p_}_{which}")
            sc.wait_ge(sea, ev.ev[f"lnln_p{p_}_{which}"][1])
            ins = sc.activation(aux2[0:1, :R], aux2[0:1, :R], AF.Exp, scale=-0.5)
            ev.inc(ins, sea, 1, f"lnsmall_p{p_}_{which}")

        @block.gpsimd
        def _(g):
            for i, (lab, src) in enumerate(fill_seq):
                for p_ in range(NPASS):
                    if lab == f"up_p{p_}_0":
                        if p_ > 0:
                            ev.wait(g, f"sum_p{p_-1}_1")
                        ev.inc(g.dma_start(out=xb[:, :], in_=xt[p_]), swq, 16,
                               f"xbload_p{p_}")
                if i >= NPOOL:
                    ev.wait(g, f"wdone_{fill_seq[i - NPOOL][0]}")
                dst = wslot(lab)
                if src.ndim == 3:
                    c2 = src.shape[1]
                    fr = c2 * src.shape[2]
                    ev.inc(g.dma_start(
                        out=dst[:, :fr].rearrange("p (c f) -> p c f", c=c2),
                        in_=src), swq, 16, f"wq_{lab}")
                else:
                    pp, fr = src.shape
                    ev.inc(g.dma_start(out=dst[:pp, :fr], in_=src), swq, 16,
                           f"wq_{lab}")

        @block.sync
        def _(s):
            ev.inc(s.dma_start(out=bias_sb[:, :], in_=biasp), sxq, 16)
            ev.inc(s.dma_start(out=lp_sb[:, :], in_=lpp), sxq, 16)
            ev.inc(s.dma_start(out=df_sb[:, :], in_=dfp), sxq, 16)
            ev.inc(s.dma_start(out=lsi_sb[:, :], in_=lsip), sxq, 16)
            ev.inc(s.dma_start(out=bid_sb[:, :], in_=bidp), sxq, 16, "consts_in")
            for p_ in range(NPASS):
                if p_ > 0:
                    s.wait_ge(sout, 16 * p_)
                ev.inc(s.dma_start(out=acc[:, :], in_=xt[p_]), sxq, 16,
                       f"accload_p{p_}")
                ev.wait(s, f"ln2_done_p{p_}")
                ev.inc(s.dma_start(out=yout[p_], in_=acc[:, :]), sout, 16,
                       f"yout_p{p_}")
            s.wait_ge(sout, 16 * NPASS)

        @block.tensor
        def _(t):
            def group(bank, mms, glabel, waits=()):
                for w in waits:
                    ev.wait(t, w)
                n = len(mms)
                ins = None
                for q, (lhs, rhs) in enumerate(mms):
                    ins = t.matmul(bank[:, :R], lhs, rhs,
                                   start=(q == 0), stop=(q == n - 1))
                return ev.inc(ins, smm, 1, glabel)

            ev.wait(t, "mvec_done")
            ins = t.matmul(pS[0][:, 0:4], onesf[0:1, :], mvec[0:1, :],
                           start=True, stop=True)
            ev.inc(ins, smm, 1, "mrep4")

            for p_ in range(NPASS):
                for s_ in range(SB):
                    lab, fl = f"au_p{p_}_{s_}", f"up_p{p_}_{s_}"
                    bank_wait(t, lab, seqA, idxA)
                    group(bankA(lab),
                          [(wslot(fl)[:, k * P:(k + 1) * P],
                            xb[:, k * R:(k + 1) * R]) for k in range(KH)],
                          lab, waits=[f"wq_{fl}", f"xbload_p{p_}"])
                    ev.alias(f"wdone_{fl}", lab)
                for s_ in range(SB):
                    lab, fl = f"gp1_p{p_}_{s_}", f"g1w_p{p_}_{s_}"
                    bank_wait(t, lab, seqA, idxA)
                    group(bankA(lab),
                          [(wslot(fl)[:, k * P:(k + 1) * P],
                            xb[:, k * R:(k + 1) * R]) for k in range(KH)],
                          lab, waits=[f"wq_{fl}"])
                    ev.alias(f"wdone_{fl}", lab)
                for c in range(CB):
                    lab, fl = f"G1_p{p_}_{c}", f"a1_p{p_}_{c}"
                    bank_wait(t, lab, seqA, idxA)
                    sl = wslot(fl)
                    rhs_bufs = [aub, tmp[0], tmp[1], tmp[2]]
                    mms = []
                    for w in range(G1W):
                        for k in range(SB):
                            mms.append((sl[:, (w * SB + k) * P:(w * SB + k + 1) * P],
                                        rhs_bufs[w][:, k * R:(k + 1) * R]))
                    group(bankA(lab), mms, lab, waits=[f"wq_{fl}", f"tmpg1_p{p_}"])
                    ev.alias(f"wdone_{fl}", lab)
                for s_ in range(SB):
                    lab, fl = f"gp2_p{p_}_{s_}", f"g2w_p{p_}_{s_}"
                    bank_wait(t, lab, seqA, idxA)
                    group(bankA(lab),
                          [(wslot(fl)[:, k * P:(k + 1) * P],
                            xb[:, k * R:(k + 1) * R]) for k in range(KH)],
                          lab, waits=[f"wq_{fl}"])
                    ev.alias(f"wdone_{fl}", lab)
                for c in range(CB):
                    lab, fl = f"G2_p{p_}_{c}", f"a2_p{p_}_{c}"
                    bank_wait(t, lab, seqA, idxA)
                    sl = wslot(fl)
                    mms = []
                    for w in range(G2W):
                        for k in range(SB):
                            mms.append((sl[:, (w * SB + k) * P:(w * SB + k + 1) * P],
                                        tmp[w][:, k * R:(k + 1) * R]))
                    group(bankA(lab), mms, lab, waits=[f"wq_{fl}", f"tmpg2_p{p_}"])
                    ev.alias(f"wdone_{fl}", lab)
                for fc in range(NFC):
                    for j in range(FBC):
                        labg, flg = f"g_p{p_}_{fc}_{j}", f"gw_p{p_}_{fc}_{j}"
                        labu, flu = f"u_p{p_}_{fc}_{j}", f"uw_p{p_}_{fc}_{j}"
                        bank_wait(t, labg, seqG, idxG)
                        group(pG[idxG[labg] % 2],
                              [(wslot(flg)[:, k * P:(k + 1) * P],
                                xb[:, k * R:(k + 1) * R]) for k in range(KH)],
                              labg, waits=[f"wq_{flg}"])
                        ev.alias(f"wdone_{flg}", labg)
                        bank_wait(t, labu, seqU, idxU)
                        group(pU[idxU[labu] % 2],
                              [(wslot(flu)[:, k * P:(k + 1) * P],
                                xb[:, k * R:(k + 1) * R]) for k in range(KH)],
                              labu, waits=[f"wq_{flu}"])
                        ev.alias(f"wdone_{flu}", labu)
                    hbuf = hb0 if fc % 2 == 0 else hb1
                    for cp in range(CB // 2):
                        fl = f"dw_p{p_}_{fc}_{cp}"
                        for ci in range(2):
                            c = 2 * cp + ci
                            lab = f"D_p{p_}_{fc}_{c}"
                            bank_wait(t, lab, seqA, idxA)
                            sl = wslot(fl)
                            mms = [(sl[:, (ci * FBC + j) * P:(ci * FBC + j + 1) * P],
                                    hbuf[:, j * R:(j + 1) * R]) for j in range(FBC)]
                            group(bankA(lab), mms, lab,
                                  waits=[f"wq_{fl}", f"hready_p{p_}_{fc}"])
                        ev.alias(f"wdone_{fl}", f"D_p{p_}_{fc}_{2*cp+1}")
                ln_mm(t, p_, 0)
                ev.wait(t, f"xpb_p{p_}")
                ev.wait(t, "apb_done")
                bank_wait(t, f"pmm_p{p_}", seqS0, idxS0, depth=1)
                ins = None
                for k in range(KH):
                    ins = t.matmul(pS[0][0:LR, :R], apb[:, k * LR:(k + 1) * LR],
                                   xb[:, k * R:(k + 1) * R],
                                   start=(k == 0), stop=(k == KH - 1))
                ev.inc(ins, smm, 1, f"pmm_p{p_}")
                for c in range(CB):
                    lab, fl = f"F_p{p_}_{c}", f"wt_p{p_}_{c}"
                    bank_wait(t, lab, seqA, idxA)
                    sl = wslot(fl)
                    mms = [(sl[:, k * P:(k + 1) * P], xb[:, k * R:(k + 1) * R])
                           for k in range(KH)]
                    mms.append((bpb[0:LR, c * P:(c + 1) * P], pb[0:LR, :R]))
                    group(bankA(lab), mms, lab,
                          waits=[f"wq_{fl}", f"pb_p{p_}", "bpb_done"])
                    ev.alias(f"wdone_{fl}", lab)
                ln_mm(t, p_, 1)

        @block.vector
        def _(v):
            v.memset(onesb[:, :], 1.0)
            v.memset(onesf[:, :], 1.0)
            v.memset(cln[:, :], LNBIAS)
            v.memset(ceps[:, :], EPS)
            v.memset(cthr[:, :], float(np.log(THD2)))
            v.memset(mvec[:, :], 0.0)
            ev.wait(v, "consts_in")
            ev.inc(v.tensor_scalar(negdf[:, :], df_sb[:, :], -1.0, None,
                                   op0=ALU.mult), sev, 1, "negdf_done")
            chn = [0]

            def vch(ins):
                chn[0] += 1
                ev.inc(ins, sev, 1, f"vchain_{chn[0]}")
                v.wait_ge(sev, ev.ev[f"vchain_{chn[0]}"][1])
                return ins

            vch(v.tensor_copy(lsf_sb[:, :], lsi_sb[:, :]))
            for j in range(4):
                vch(v.tensor_tensor(cvec[0:1, j:j + 1], bid_sb[0:1, 0:1],
                                    lsf_sb[0:1, j:j + 1], op=ALU.is_lt))
            vch(v.tensor_tensor(svec[0:1, 0:1], cvec[0:1, 1:2], cvec[0:1, 0:1],
                                op=ALU.mult))
            vch(v.tensor_scalar(svec[0:1, 1:2], cvec[0:1, 1:2], -1.0, 2.0,
                                op0=ALU.mult, op1=ALU.add))
            vch(v.tensor_tensor(svec[0:1, 1:2], svec[0:1, 1:2], svec[0:1, 0:1],
                                op=ALU.subtract))
            vch(v.tensor_scalar(svec[0:1, 2:3], cvec[0:1, 3:4], -1.0, 4.0,
                                op0=ALU.mult, op1=ALU.add))
            vch(v.tensor_tensor(svec[0:1, 3:4], svec[0:1, 1:2], svec[0:1, 2:3],
                                op=ALU.subtract))
            vch(v.tensor_tensor(svec[0:1, 3:4], cvec[0:1, 2:3], svec[0:1, 3:4],
                                op=ALU.mult))
            vch(v.tensor_tensor(svec[0:1, 4:5], svec[0:1, 2:3], svec[0:1, 3:4],
                                op=ALU.add))
            for k in range(NAD):
                vch(v.tensor_scalar(mvec[0:1, k:k + 1], svec[0:1, 4:5],
                                    float(k + 2), None, op0=ALU.is_equal))
            ev.alias("mvec_done", f"vchain_{chn[0]}")
            ev.wait(v, "mrep4")
            vch(v.tensor_copy(mrep4[:, :], pS[0][:, 0:4]))
            ev.alias("cons_mrep4", f"vchain_{chn[0]}")
            ev.wait(v, "wq_at0")
            vch(v.tensor_scalar(apb[:, :], wslot("at0")[:, :KH * LR],
                                mrep4[:, 0:1], None, op0=ALU.mult))
            for k in (1, 2):
                ev.wait(v, f"wq_at{k}")
                vch(v.tensor_scalar(ab2[:, :], wslot(f"at{k}")[:, :KH * LR],
                                    mrep4[:, k:k + 1], None, op0=ALU.mult))
                vch(v.tensor_tensor(apb[:, :], apb[:, :], ab2[:, :],
                                    op=ALU.add))
            ev.alias("apb_done", f"vchain_{chn[0]}")
            ev.wait(v, "wq_bt0")
            vch(v.tensor_scalar(bpb[0:LR, :], wslot("bt0")[0:LR, :H],
                                mrep4[0:LR, 0:1], None, op0=ALU.mult))
            for k in (1, 2):
                ev.wait(v, f"wq_bt{k}")
                vch(v.tensor_scalar(bb2[0:LR, :], wslot(f"bt{k}")[0:LR, :H],
                                    mrep4[0:LR, k:k + 1], None, op0=ALU.mult))
                vch(v.tensor_tensor(bpb[0:LR, :], bpb[0:LR, :],
                                    bb2[0:LR, :], op=ALU.add))
            ev.alias("bpb_done", f"vchain_{chn[0]}")
            for k in range(NAD):
                ev.alias(f"wdone_at{k}", "apb_done")
                ev.alias(f"wdone_bt{k}", "bpb_done")

            for p_ in range(NPASS):
                for gp in (1, 2):
                    mlist = (1, 2, 3) if gp == 1 else (4, 5, 6)
                    for s_ in range(SB):
                        lab = f"gp{gp}_p{p_}_{s_}"
                        q = gate_q[(p_, gp, s_)]
                        ev.wait(v, lab)
                        if q >= 2:
                            pq = gate_chunks[q - 2]
                            ev.wait(v, f"lnchunk_p{pq[0]}_{pq[1]}_{pq[2]}")
                        ins = v.tensor_scalar(d32[s_ % 2][:, :R], bankA(lab)[:, :R],
                                              lp_sb[:, s_:s_ + 1], None,
                                              op0=ALU.subtract)
                        ev.inc(ins, sev, 1, f"cons_{lab}")
                        ins = v.tensor_tensor(s32[s_ % 2][:, :R], d32[s_ % 2][:, :R],
                                              d32[s_ % 2][:, :R], op=ALU.mult)
                        ev.inc(ins, sev, 1, f"ssq_p{p_}_{gp}_{s_}")
                        for m in mlist:
                            ti = tmp[(m - 1) % 3]
                            tsl = ti[:, s_ * R:(s_ + 1) * R]
                            ev.wait(v, f"exp_p{p_}_{m}_{s_}")
                            if m == mlist[0] and s_ == 0:
                                if gp == 1:
                                    if p_ == 0:
                                        ev.wait(v, "bpb_done")
                                    else:
                                        ev.wait(v, f"D_p{p_-1}_{NFC-1}_{CB-1}")
                                else:
                                    ev.wait(v, f"G1_p{p_}_{CB-1}")
                            first = True
                            if m in (5, 6):
                                v.tensor_tensor(kps[:, :R], lgsb[s_ % 2][:, :R],
                                                cthr[:, 0:1].broadcast_to([P, R]),
                                                op=ALU.is_le)
                                v.tensor_tensor(tsl, kps[:, :R], ebs[m % 2][:, :R],
                                                op=ALU.mult)
                                first = False
                            if m % 2 == 1:
                                src = ebs[m % 2][:, :R] if first else tsl
                                v.tensor_tensor(tsl, src, negsb[s_ % 2][:, :R],
                                                op=ALU.mult)
                                first = False
                            src = ebs[m % 2][:, :R] if first else tsl
                            ins = v.tensor_tensor(tsl, src,
                                                  aub[:, s_ * R:(s_ + 1) * R],
                                                  op=ALU.mult)
                            ev.inc(ins, sev, 1, f"tmpm_p{p_}_{m}_{s_}")
                    ev.alias(f"tmpg{gp}_p{p_}", f"tmpm_p{p_}_{mlist[-1]}_{SB-1}")
                    glab = "G1" if gp == 1 else "G2"
                    for c in range(CB):
                        lab = f"{glab}_p{p_}_{c}"
                        ev.wait(v, lab)
                        if glab == "G1" and c == 0:
                            ev.wait(v, f"accload_p{p_}")
                        ins = v.tensor_tensor(acc[:, c * R:(c + 1) * R],
                                              acc[:, c * R:(c + 1) * R],
                                              bankA(lab)[:, :R], op=ALU.add)
                        ev.inc(ins, sev, 1, f"cons_{lab}")
                for fc in range(NFC):
                    hbuf = hb0 if fc % 2 == 0 else hb1
                    for j in range(FBC):
                        labu = f"u_p{p_}_{fc}_{j}"
                        ev.wait(v, f"silu_p{p_}_{fc}_{j}")
                        ev.wait(v, labu)
                        if j == 0:
                            if fc >= 2:
                                ev.wait(v, f"D_p{p_}_{fc-2}_{CB-1}")
                            elif p_ > 0:
                                ev.wait(v, f"D_p{p_-1}_{NFC-2+fc}_{CB-1}")
                            else:
                                ev.wait(v, f"G2_p0_{CB-1}")
                        ins = v.tensor_tensor(hbuf[:, j * R:(j + 1) * R],
                                              sg32[:, :R],
                                              pU[idxU[labu] % 2][:, :R],
                                              op=ALU.mult)
                        ev.inc(ins, sev, 1, f"cons_{labu}")
                    ev.alias(f"hready_p{p_}_{fc}", f"cons_u_p{p_}_{fc}_{FBC-1}")
                    for c in range(CB):
                        lab = f"D_p{p_}_{fc}_{c}"
                        ev.wait(v, lab)
                        ins = v.tensor_tensor(acc[:, c * R:(c + 1) * R],
                                              acc[:, c * R:(c + 1) * R],
                                              bankA(lab)[:, :R], op=ALU.add)
                        ev.inc(ins, sev, 1, f"cons_{lab}")
                ln_vec(v, p_, 0)
                for c in range(CB):
                    lab = f"F_p{p_}_{c}"
                    ev.wait(v, lab)
                    ins = v.tensor_scalar(scrF[:, :R], bankA(lab)[:, :R],
                                          bias_sb[:, c:c + 1], None, op0=ALU.add)
                    ev.inc(ins, sev, 1, f"cons_{lab}")
                    ins = v.tensor_tensor(acc[:, c * R:(c + 1) * R],
                                          acc[:, c * R:(c + 1) * R],
                                          scrF[:, :R], op=ALU.add)
                    ev.inc(ins, sev, 1, f"evf_p{p_}_{c}")
                ln_vec(v, p_, 1)

        @block.scalar
        def _(sc):
            for p_ in range(NPASS):
                for s_ in range(SB):
                    lab = f"au_p{p_}_{s_}"
                    ev.wait(sc, lab)
                    ins = sc.copy(aub[:, s_ * R:(s_ + 1) * R], bankA(lab)[:, :R])
                    ev.inc(ins, sea, 1, f"cons_{lab}")
                for gp in (1, 2):
                    mlist = (1, 2, 3) if gp == 1 else (4, 5, 6)
                    for s_ in range(SB):
                        ev.wait(sc, f"cons_gp{gp}_p{p_}_{s_}")
                        ins = sc.activation(negsb[s_ % 2][:, :R],
                                            d32[s_ % 2][:, :R], AF.Sign)
                        ev.inc(ins, sea, 1, f"sign_p{p_}_{gp}_{s_}")
                        ev.wait(sc, f"ssq_p{p_}_{gp}_{s_}")
                        if p_ == 0 and gp == 1 and s_ == 0:
                            ev.wait(sc, "negdf_done")
                        ins = sc.activation(lgsb[s_ % 2][:, :R],
                                            s32[s_ % 2][:, :R], AF.Ln,
                                            bias=cln[:, :])
                        ev.inc(ins, sea, 1, f"lnchunk_p{p_}_{gp}_{s_}")
                        for m in mlist:
                            if m == mlist[2]:
                                ev.wait(sc, f"tmpm_p{p_}_{mlist[0]}_{s_}")
                            ins = sc.activation(ebs[m % 2][:, :R],
                                                lgsb[s_ % 2][:, :R], AF.Exp,
                                                bias=negdf[:, m - 1:m],
                                                scale=float(m) / 2.0)
                            ev.inc(ins, sea, 1, f"exp_p{p_}_{m}_{s_}")
                for fc in range(NFC):
                    for j in range(FBC):
                        labg = f"g_p{p_}_{fc}_{j}"
                        ev.wait(sc, labg)
                        if not (fc == 0 and j == 0):
                            pfc, pj = (fc, j - 1) if j > 0 else (fc - 1, FBC - 1)
                            ev.wait(sc, f"cons_u_p{p_}_{pfc}_{pj}")
                        elif p_ > 0:
                            ev.wait(sc, f"cons_u_p{p_-1}_{NFC-1}_{FBC-1}")
                        ins = sc.activation(sg32[:, :R], pG[idxG[labg] % 2][:, :R],
                                            AF.Silu)
                        ev.inc(ins, sea, 1, f"silu_p{p_}_{fc}_{j}")
                        ev.alias(f"cons_{labg}", f"silu_p{p_}_{fc}_{j}")
                ln_scalar(sc, p_, 0)
                ev.wait(sc, f"ln1_norm_p{p_}")
                ins = sc.copy(xb[:, :CB * R], acc[:, :])
                ev.inc(ins, sea, 1, f"xpb_p{p_}")
                ev.wait(sc, f"pmm_p{p_}")
                ins = sc.copy(pb[0:LR, :R], pS[0][0:LR, :R])
                ev.inc(ins, sea, 1, f"pb_p{p_}")
                ev.alias(f"cons_pmm_p{p_}", f"pb_p{p_}")
                ln_scalar(sc, p_, 1)

    if known_ev is None:
        return None, dict(ev.ev)
    return nc, None


def build(d):
    _, known = build_program(d, known_ev=None)
    nc, _ = build_program(d, known_ev=known)
    return nc



def pack_all(inp, d):
    d = derived(d)
    KH, SB, FBC, NFC = d["KH"], d["SB"], d["FBC"], d["NFC"]
    R, NPASS, H = d["R"], d["NPASS"], d["H"]
    CB = KH
    C = np.ascontiguousarray

    def packA(w):
        ob, ib = w.shape
        return C(w.reshape(ob // P, P, ib // P, P).transpose(0, 3, 2, 1)
                 .reshape(ob // P, P, ib))

    def packB(ws):
        out = np.empty((CB, P, len(ws) * SB * P), np.float32)
        for wi, w in enumerate(ws):
            t = w.reshape(CB, P, SB, P).transpose(0, 3, 2, 1)
            out[:, :, wi * SB * P:(wi + 1) * SB * P] = t.reshape(CB, P, SB * P)
        return C(out)

    f32 = lambda k: np.asarray(inp[k], np.float32)
    shared = {}
    shared["upw"] = packA(f32("L0_attn_up_w"))
    shared["gatew"] = packA(f32("L0_attn_gate_w"))
    lao = f32("L0_attn_local_approx_output")
    fuse = f32("L0_attn_fuse_weight")
    shared["apg1"] = packB([lao] + [fuse[:, :, m] for m in range(3)])
    shared["apg2"] = packB([fuse[:, :, m] for m in range(3, 6)])
    shared["gwp"] = packA(f32("L0_attn_ffn_gate_w"))
    shared["uwp"] = packA(f32("L0_attn_ffn_up_w"))
    dw = f32("L0_attn_ffn_down_w")
    shared["dwp"] = C(dw.reshape(CB, P, NFC, FBC, P).transpose(2, 0, 4, 3, 1)
                      .reshape(NFC, CB, P, FBC * P))
    shared["wtp"] = packA(f32("L0_ffn_weight"))
    A = f32("L0_ffn_lora_A")
    shared["atp"] = C(A.transpose(0, 2, 1).reshape(NAD, KH, P, LR)
                      .transpose(0, 2, 1, 3)).reshape(NAD, P, KH * LR)
    shared["btp"] = C(f32("L0_ffn_lora_B").transpose(0, 2, 1))
    shared["biasp"] = C(f32("L0_ffn_bias").reshape(CB, P).T)
    shared["lpp"] = C(f32("L0_attn_local_point").reshape(SB, P).T)
    shared["dfp"] = C(np.broadcast_to(
        f32("L0_attn_discount_factor")[None, :], (P, G)))
    lsi = np.asarray(inp["L0_ffn_lora_start_indices"])
    shared["lsip"] = C(lsi.astype(np.int32, copy=False).reshape(1, 4))

    x = f32("L0_attn_hidden_states")
    xr = x.reshape(d["B"] * d["T"], H)
    in_maps = []
    for i in range(d["NCORES"]):
        xc = xr[i * d["ROWS_CORE"]:(i + 1) * d["ROWS_CORE"]]
        xtp = C(xc.reshape(NPASS, R, KH, P).transpose(0, 3, 2, 1)).reshape(
            NPASS, P, KH * R)
        m = dict(shared)
        m["xt"] = xtp
        m["bidp"] = np.array([[float(i // (d["NCORES"] // d["B"]))]], np.float32)
        in_maps.append(m)
    return in_maps


def unpack_out(res, d):
    d = derived(d)
    KH, R, NPASS, H = d["KH"], d["R"], d["NPASS"], d["H"]
    outs = []
    for i in range(d["NCORES"]):
        y = np.asarray(res.results[i]["y"])
        outs.append(y.reshape(NPASS, P, KH, R).transpose(0, 3, 2, 1)
                    .reshape(NPASS * R, H))
    full = np.concatenate(outs, axis=0)
    return np.ascontiguousarray(full.reshape(d["B"], d["T"], H)).astype(
        np.float32, copy=False)


def kernel(**inputs):
    d = full_dims()
    in_maps = pack_all(inputs, d)
    nc = build(d)
    res = run_bass_kernel_spmd(nc, in_maps, core_ids=list(range(d["NCORES"])))
    return unpack_out(res, d)
